# version 15
# baseline (speedup 1.0000x reference)
import numpy as np
import concourse.bass as bass
import concourse.mybir as mybir
from concourse.bass_utils import run_bass_kernel_spmd
from contextlib import ExitStack

F32 = mybir.dt.float32
BF16 = mybir.dt.bfloat16
AF = mybir.ActivationFunctionType
OP = mybir.AluOpType
AX = mybir.AxisListType
f32 = np.float32

D = 2048
HID = 8192
CAP = 30.0
SCALE = 1.0 / np.sqrt(128.0)
EPS = 1e-5
NT = 16
NTO = 8
NHG = 4
GA = 2
NGRP = 64 // GA
NEG = -100.0
SC_HI = float(f32(1.0 / (np.sqrt(128.0) * 30.0)))
SC_LO = float(f32(1.0 / (np.sqrt(np.float64(128.0)) * np.float64(30.0)) - np.float64(f32(1.0 / (np.sqrt(128.0) * 30.0)))))
FW_HI = float(f32(1.0 / np.sqrt(8192.0)))
FW_LO = float(f32(1.0 / np.sqrt(np.float64(8192.0)) - np.float64(f32(1.0 / np.sqrt(8192.0)))))


class Sem:
    def __init__(self, stack, nc, name):
        self.h = stack.enter_context(nc.semaphore(name))
        self.n = 0

    def inc(self, instr, k=1):
        instr.then_inc(self.h, k)
        self.n += k
        return self.n


class DmaQ:

    def __init__(self, stack, nc, name, n=12):
        self.lanes = [Sem(stack, nc, "%s%d" % (name, i)) for i in range(n)]
        self.i = 0

    def issue(self, eng, out, in_, **kw):
        ln = self.lanes[self.i % len(self.lanes)]
        self.i += 1
        if ln.n:
            eng.wait_ge(ln.h, ln.n)
        ln.inc(eng.dma_start(out=out, in_=in_, **kw), 16)
        return (ln.h, ln.n)

    def snapshot(self):
        return [(l.h, l.n) for l in self.lanes if l.n]


def wait_toks(eng, toks):
    best = {}
    for h, v in toks:
        key = id(h)
        if key not in best or best[key][1] < v:
            best[key] = (h, v)
    for h, v in best.values():
        eng.wait_ge(h, v)


class Builder:
    def __init__(self, phases=("p0", "p1", "p2p3", "p4", "p5")):
        self.nc = bass.Bass("TRN2", target_bir_lowering=False)
        self.stack = ExitStack()
        self.phases = phases

    def sb(self, name, shape):
        return self.stack.enter_context(self.nc.sbuf_tensor(name, shape, F32))

    def ps(self, name, shape):
        return self.stack.enter_context(self.nc.psum_tensor(name, shape, F32))

    def build(self):
        nc = self.nc
        dt = nc.dram_tensor
        self.xall = dt("xall", [NT, 128, D], F32, kind="ExternalInput").ap()
        self.wqT = dt("wqT", [16, 128, D], F32, kind="ExternalInput").ap()
        self.wkT = dt("wkT", [16, 128, D], F32, kind="ExternalInput").ap()
        self.wvT = dt("wvT", [16, 128, D], F32, kind="ExternalInput").ap()
        self.woT = dt("woT", [16, 128, D], F32, kind="ExternalInput").ap()
        self.w1T = dt("w1T", [16, 128, HID], F32, kind="ExternalInput").ap()
        self.w3T = dt("w3T", [16, 128, HID], F32, kind="ExternalInput").ap()
        self.w2r = dt("w2r", [64, 128 * D], F32, kind="ExternalInput").ap()
        self.h64c = dt("h64c", [64, 64], F32, kind="ExternalInput").ap()
        self.h128c = dt("h128c", [128, 128], F32, kind="ExternalInput").ap()
        self.identc = dt("identc", [128, 128], F32, kind="ExternalInput").ap()
        self.biasc = dt("biasc", [128, 256], F32, kind="ExternalInput").ap()
        self.validc = dt("validc", [128, 72], F32, kind="ExternalInput").ap()
        self.y = dt("y", [NTO, 128, D], F32, kind="ExternalOutput").ap()
        self.h1d = dt("h1d", [16, 128, D], F32, kind="Internal").ap()
        self.v2dh = dt("v2dh", [64, 128, D], BF16, kind="Internal").ap()
        self.v2dl = dt("v2dl", [64, 128, D], BF16, kind="Internal").ap()
        self.x2d = dt("x2d", [NTO, 128, D], F32, kind="Internal").ap()
        self.v2d = dt("v2d", [64, 128, D], F32, kind="Internal").ap()

        self.c_ident = self.sb("c_ident", [128, 128])
        self.c_h128 = self.sb("c_h128", [128, 128])
        self.c_h64 = self.sb("c_h64", [64, 64])
        self.c_h64b = self.sb("c_h64b", [128, 64])
        self.c_bias = self.sb("c_bias", [128, 256])
        self.c_valid = self.sb("c_valid", [128, 72])
        self.big = self.sb("big", [128, 24576])
        self.Ksb = self.big[:, 0:8192]
        self.Vsb = self.big[:, 8192:16384]
        self.Qsb = self.big[:, 16384:20480]
        self.wob = [self.big[:, 20480:22528], self.big[:, 22528:24576]]
        self.w1g = self.big[:, 0:4096]
        self.w3g = self.big[:, 4096:8192]
        self.v2g = [self.big[:, 8192:12288], self.big[:, 12288:16384]]
        self.h2f = self.big[:, 16384:24576]

        self.oacc = self.sb("oacc", [128, 16384])
        self.scrA = self.sb("scrA", [128, 2048])
        self.scrB = self.sb("scrB", [128, 2048])
        self.scrC = self.sb("scrC", [128, 2048])
        self.h1b = [self.sb("h1b%d" % i, [128, 512]) for i in range(3)]
        self.wb1 = [self.sb("wb1%d" % i, [128, 512]) for i in range(3)]
        self.wb2 = [self.sb("wb2%d" % i, [128, 512]) for i in range(3)]
        self.atm = [self.sb("atm0", [128, 128]), self.sb("atm1", [128, 128])]
        self.afm = [self.sb("afm0", [128, 128]), self.sb("afm1", [128, 128])]
        self.stats = self.sb("stats", [128, 8])
        self.sgb = [self.scrA[:, 0:512], self.scrA[:, 512:1024]]
        self.hidb = [self.scrA[:, 1024:1536], self.scrA[:, 1536:2048]]
        self.tfn = [self.scrC[:, 0:512], self.scrC[:, 512:1024]]

        self.psA = self.ps("psA", [128, 2048])
        self.psB = self.ps("psB", [128, 2048])

        st = self.stack
        self.ldq = DmaQ(st, nc, "ld", 12)
        self.sgq = DmaQ(st, nc, "sg", 8)
        self.ts = Sem(st, nc, "ts")
        self.vs = Sem(st, nc, "vs")
        self.ss = Sem(st, nc, "ss")

        self.strm_last = [0, 0, 0]
        self.wo_rd = [0, 0]

        self.block = st.enter_context(nc.Block())
        self._load_consts()
        if "p0" in self.phases:
            self.p0_v2()
        if "p1" in self.phases:
            self.p1_h1()
        if "p2p3" in self.phases:
            for hg in range(NHG):
                self.p2_kvq(hg)
                self.p3_attn(hg)
        if "p4" in self.phases:
            self.p4_x2()
        if "p5" in self.phases:
            for hf in range(2):
                self.p5_ffn_half(hf)
        self._finish()
        return nc

    def _load_consts(self):
        b = self.block
        toks = []

        @b.sync
        def _(s):
            toks.append(self.ldq.issue(s, self.c_ident[:], self.identc[:]))
            toks.append(self.ldq.issue(s, self.c_h128[:], self.h128c[:]))
            toks.append(self.ldq.issue(s, self.c_h64[:], self.h64c[:]))
            toks.append(self.ldq.issue(s, self.c_h64b[64:128, :], self.h64c[:]))
            toks.append(self.ldq.issue(s, self.c_bias[:], self.biasc[:]))
            toks.append(self.ldq.issue(s, self.c_valid[:], self.validc[:]))

        for deco in (b.tensor, b.vector, b.scalar, b.gpsimd):
            @deco
            def _(e):
                wait_toks(e, toks)


    def barrier(self):
        b = self.block
        vals = ([(s.h, s.n) for s in (self.ts, self.vs, self.ss) if s.n]
                + self.ldq.snapshot() + self.sgq.snapshot())

        for deco in (b.sync, b.tensor, b.vector, b.scalar, b.gpsimd):
            @deco
            def _(e, vals=vals):
                for h, n in vals:
                    e.wait_ge(h, n)

    def p0_v2(self):
        b = self.block
        ts, vs, ss = self.ts, self.vs, self.ss
        w2b = [self.oacc[:, 0:2048], self.oacc[:, 2048:4096]]
        obh = [self.oacc[:, 4096:5120].bitcast(BF16), self.oacc[:, 6144:7168].bitcast(BF16)]
        obl = [self.oacc[:, 5120:6144].bitcast(BF16), self.oacc[:, 7168:8192].bitcast(BF16)]
        pso = [self.psA[0:64, :], self.psB[0:64, :]]
        tok_ld = [None] * 64
        tok_st = [None] * 64
        mm_sn = [0] * 64
        ev_sn = [0] * 64
        ev_sn2 = [0] * 64

        for i in range(64):
            p = i % 2
            ke, ko = 2 * i, 2 * i + 1

            @b.sync
            def _(s, i=i, p=p, ke=ke, ko=ko):
                if i >= 2:
                    s.wait_ge(ts.h, mm_sn[i - 2])
                tok_ld[i] = [
                    self.ldq.issue(s, w2b[p][0:64, :], self.w2r[:, ke * D:(ke + 1) * D]),
                    self.ldq.issue(s, w2b[p][64:128, :], self.w2r[:, ko * D:(ko + 1) * D]),
                ]

            @b.tensor
            def _(t, i=i, p=p):
                wait_toks(t, tok_ld[i])
                if i >= 1:
                    t.wait_ge(vs.h, ev_sn[i - 1])
                    t.wait_ge(ss.h, ev_sn2[i - 1])
                mm = None
                for j in range(4):
                    t.matmul(pso[0][:, j * 512:(j + 1) * 512],
                             self.c_h64[:, :],
                             w2b[p][0:64, j * 512:(j + 1) * 512],
                             start=True, stop=True)
                    mm = t.matmul(pso[1][:, j * 512:(j + 1) * 512],
                                  self.c_h64b[64:128, :],
                                  w2b[p][64:128, j * 512:(j + 1) * 512],
                                  start=True, stop=True)
                mm_sn[i] = ts.inc(mm)

            @b.scalar
            def _(s, i=i, p=p):
                s.wait_ge(ts.h, mm_sn[i])
                if i >= 2:
                    wait_toks(s, tok_st[i - 2])
                ss.inc(s.copy(obh[p][0:64, :], pso[0]))
                ev_sn2[i] = ss.inc(s.copy(obh[p][64:128, :], pso[1]))

            @b.vector
            def _(v, i=i, p=p):
                v.wait_ge(ts.h, mm_sn[i])
                v.wait_ge(ss.h, ev_sn2[i])
                if i >= 2:
                    wait_toks(v, tok_st[i - 2])
                vs.inc(v.tensor_tensor(obl[p][0:64, :], pso[0], obh[p][0:64, :],
                                       OP.subtract))
                ev_sn[i] = vs.inc(v.tensor_tensor(obl[p][64:128, :], pso[1],
                                                  obh[p][64:128, :], OP.subtract))

            @b.gpsimd
            def _(g, i=i, p=p, ke=ke, ko=ko):
                g.wait_ge(vs.h, ev_sn[i])
                tok_st[i] = [
                    self.sgq.issue(g, self.v2dh[:, ke, :], obh[p][0:64, :]),
                    self.sgq.issue(g, self.v2dh[:, ko, :], obh[p][64:128, :]),
                    self.sgq.issue(g, self.v2dl[:, ke, :], obl[p][0:64, :]),
                    self.sgq.issue(g, self.v2dl[:, ko, :], obl[p][64:128, :]),
                ]

        self.barrier()

    def emit_ln(self, src, dst, resid=None, n=D):
        b = self.block
        stx = self.stats
        vs, ss = self.vs, self.ss
        r = {}

        @b.vector
        def _(v):
            r["a"] = vs.inc(v.reduce_sum(stx[:, 0:1], src, axis=AX.X))

        @b.scalar
        def _(s):
            r["b"] = ss.inc(s.activation(self.scrB[:, 0:n], src, AF.Square,
                                         accum_out=stx[:, 1:2]))

        @b.vector
        def _(v):
            v.wait_ge(vs.h, r["a"])
            v.tensor_scalar(stx[:, 2:3], stx[:, 0:1], float(1.0 / n), None, OP.mult)
            v.wait_ge(ss.h, r["b"])
            c = vs.inc(v.tensor_scalar(stx[:, 3:4], stx[:, 1:2], float(1.0 / n), None, OP.mult))
            v.wait_ge(vs.h, c)
            d = vs.inc(v.tensor_tensor(stx[:, 6:7], stx[:, 2:3], stx[:, 2:3], OP.mult))
            v.wait_ge(vs.h, d)
            r["e"] = vs.inc(v.tensor_tensor(stx[:, 3:4], stx[:, 3:4], stx[:, 6:7], OP.subtract))

        @b.scalar
        def _(s):
            s.wait_ge(vs.h, r["e"])
            r["f"] = ss.inc(s.activation(stx[:, 4:5], stx[:, 3:4], AF.Sqrt, bias=f32(EPS)))

        @b.vector
        def _(v):
            v.wait_ge(ss.h, r["f"])
            g = vs.inc(v.reciprocal(stx[:, 4:5], stx[:, 4:5]))
            v.wait_ge(vs.h, g)
            h = vs.inc(v.tensor_tensor(stx[:, 5:6], stx[:, 2:3], stx[:, 4:5], OP.mult))
            v.wait_ge(vs.h, h)
            i = vs.inc(v.tensor_scalar(dst, src, stx[:, 4:5], stx[:, 5:6],
                                       OP.mult, OP.subtract))
            if resid is not None:
                v.wait_ge(vs.h, i)
                vs.inc(v.tensor_tensor(dst, dst, resid, OP.add))

    def emit_tm_to_fm(self, src, evac_to, split_to=None):
        b = self.block
        ts, vs, ss = self.ts, self.vs, self.ss
        trp = [self.psB[:, 0:128], self.psB[:, 512:640]]
        tsn = [0] * 16
        vs_base = self.vs.n
        ss_base = self.ss.n

        @b.tensor
        def _(t):
            for i in range(16):
                if i >= 2:
                    if split_to is not None:
                        t.wait_ge(vs.h, vs_base + 2 * (i - 2) + 2)
                    else:
                        j = (i - 2) // 2 + 1
                        if (i - 2) % 2 == 0:
                            t.wait_ge(vs.h, vs_base + j)
                        else:
                            t.wait_ge(ss.h, ss_base + j)
                tsn[i] = ts.inc(t.transpose(trp[i % 2], src[:, i * 128:(i + 1) * 128],
                                            self.c_ident[:]))

        if split_to is not None:
            @b.vector
            def _(v):
                for i in range(16):
                    v.wait_ge(ts.h, tsn[i])
                    a1 = vs.inc(v.tensor_copy(evac_to(i), trp[i % 2]))
                    v.wait_ge(vs.h, a1)
                    vs.inc(v.tensor_tensor(split_to(i), trp[i % 2], evac_to(i),
                                           OP.subtract))
            assert self.vs.n == vs_base + 32
        else:
            @b.vector
            def _(v):
                for i in range(0, 16, 2):
                    v.wait_ge(ts.h, tsn[i])
                    vs.inc(v.tensor_copy(evac_to(i), trp[0]))
            assert self.vs.n == vs_base + 8

            @b.scalar
            def _(s):
                for i in range(1, 16, 2):
                    s.wait_ge(ts.h, tsn[i])
                    ss.inc(s.copy(evac_to(i), trp[1]))
            assert self.ss.n == ss_base + 8
        self.fm_done_v = self.vs.n
        self.fm_done_s = self.ss.n

    def p1_h1(self):
        b = self.block
        ts, vs, ss = self.ts, self.vs, self.ss
        xb = [self.scrA, self.big[:, 0:2048]]
        ln_done = [0] * NT
        st_toks = [None] * NT
        for tt in range(NT):
            p = tt % 2
            xtok = [None]

            @b.sync
            def _(s, tt=tt, p=p):
                if tt >= 2:
                    s.wait_ge(vs.h, ln_done[tt - 2])
                xtok[0] = self.ldq.issue(s, xb[p][:, :], self.xall[tt])

            ts_now = ts.n

            @b.vector
            def _(v, tt=tt, ts_now=ts_now):
                wait_toks(v, [xtok[0]])
                if tt >= 1:
                    v.wait_ge(ts.h, ts_now)

            @b.scalar
            def _(s, tt=tt):
                wait_toks(s, [xtok[0]])
                if tt >= 1:
                    wait_toks(s, st_toks[tt - 1])

            self.emit_ln(xb[p][:, :], self.scrC[:, :])
            ln_done[tt] = vs.n
            ln_v = vs.n

            @b.tensor
            def _(t, ln_v=ln_v):
                t.wait_ge(vs.h, ln_v)

            self.emit_tm_to_fm(self.scrC, lambda i: self.scrB[:, i * 128:(i + 1) * 128])
            fmv, fms = self.fm_done_v, self.fm_done_s

            @b.gpsimd
            def _(g, tt=tt, fmv=fmv, fms=fms):
                g.wait_ge(vs.h, fmv)
                g.wait_ge(ss.h, fms)
                tl = []
                for dc in range(16):
                    tl.append(self.sgq.issue(
                        g, self.h1d[dc][:, tt * 128:(tt + 1) * 128],
                        self.scrB[:, dc * 128:(dc + 1) * 128]))
                st_toks[tt] = tl

            @b.vector
            def _(v, tt=tt):
                wait_toks(v, st_toks[tt])

        self.barrier()

    def p2_kvq(self, hg):
        b = self.block
        ts, vs, ss = self.ts, self.vs, self.ss
        psk, psv = self.psA, self.psB
        p3prev_ts = self.ts.n
        self._p2_zone_v = self.vs.n
        self._p2_zone_s = self.ss.n
        mm_done = {}
        toks = {}

        for tg in range(4):
            for d in range(16):
                p = (tg * 16 + d) % 3
                it = tg * 16 + d

                @b.sync
                def _(s, hg=hg, tg=tg, d=d, p=p, it=it):
                    s.wait_ge(ts.h, self.strm_last[p])
                    toks[it] = [
                        self.ldq.issue(s, self.h1b[p][:, :],
                                       self.h1d[d][:, tg * 512:(tg + 1) * 512]),
                        self.ldq.issue(s, self.wb1[p][:, :],
                                       self.wkT[d][:, hg * 512:(hg + 1) * 512]),
                        self.ldq.issue(s, self.wb2[p][:, :],
                                       self.wvT[d][:, hg * 512:(hg + 1) * 512]),
                    ]

                @b.tensor
                def _(t, d=d, p=p, it=it):
                    wait_toks(t, toks[it])
                    if d == 0:
                        t.wait_ge(vs.h, self._p2_zone_v)
                        t.wait_ge(ss.h, self._p2_zone_s)
                    mm = None
                    for kf in range(4):
                        t.matmul(psk[:, kf * 512:(kf + 1) * 512],
                                 self.wb1[p][:, kf * 128:(kf + 1) * 128],
                                 self.h1b[p][:, :], start=(d == 0), stop=(d == 15))
                    for vf in range(4):
                        mm = t.matmul(psv[:, vf * 512:(vf + 1) * 512],
                                      self.wb2[p][:, vf * 128:(vf + 1) * 128],
                                      self.h1b[p][:, :], start=(d == 0), stop=(d == 15))
                    mm_done[it] = ts.inc(mm)
                    self.strm_last[p] = mm_done[it]

            kv_mm = mm_done[tg * 16 + 15]

            @b.vector
            def _(v, tg=tg, kv_mm=kv_mm, p3prev_ts=p3prev_ts):
                v.wait_ge(ts.h, max(kv_mm, p3prev_ts))
                last = None
                for kf in range(4):
                    last = vs.inc(v.tensor_copy(
                        self.Ksb[:, kf * 2048 + tg * 512: kf * 2048 + (tg + 1) * 512],
                        psk[:, kf * 512:(kf + 1) * 512]))
                self._p2_kev = last

            @b.scalar
            def _(s, tg=tg, kv_mm=kv_mm, p3prev_ts=p3prev_ts):
                s.wait_ge(ts.h, max(kv_mm, p3prev_ts))
                last = None
                for vf in range(4):
                    last = ss.inc(s.copy(self.scrA[:, vf * 512:(vf + 1) * 512],
                                         psv[:, vf * 512:(vf + 1) * 512]))
                self._p2_vev = last

            vt_done = self._p2_vev
            trp = [psv[:, 0:128], psv[:, 512:640]]
            tsn = [0] * 16
            evn = [0] * 16
            eve = [None] * 16
            for i in range(16):
                vf, sub = divmod(i, 4)
                kt = tg * 4 + sub
                p = i % 2

                @b.tensor
                def _(t, i=i, p=p, vf=vf, vt_done=vt_done):
                    if i == 0:
                        t.wait_ge(ss.h, vt_done)
                    if i >= 2:
                        t.wait_ge(eve[i - 2].h, evn[i - 2])
                    tsn[i] = ts.inc(t.transpose(
                        trp[p],
                        self.scrA[:, vf * 512 + sub * 128: vf * 512 + (sub + 1) * 128],
                        self.c_ident[:]))

                dst = self.Vsb[:, kt * 512 + vf * 128: kt * 512 + (vf + 1) * 128]
                if i % 4 < 2:
                    @b.vector
                    def _(v, i=i, p=p, dst=dst):
                        v.wait_ge(ts.h, tsn[i])
                        evn[i] = vs.inc(v.tensor_copy(dst, trp[p]))
                    eve[i] = vs
                else:
                    @b.scalar
                    def _(s, i=i, p=p, dst=dst):
                        s.wait_ge(ts.h, tsn[i])
                        evn[i] = ss.inc(s.copy(dst, trp[p]))
                    eve[i] = ss
            self._p2_zone_v = self.vs.n
            self._p2_zone_s = self.ss.n

        q_mm = {}
        for tg in range(2):
            for d in range(16):
                p = (tg * 16 + d) % 3
                it = tg * 16 + d

                @b.sync
                def _(s, hg=hg, tg=tg, d=d, p=p, it=it):
                    s.wait_ge(ts.h, self.strm_last[p])
                    toks[it] = [
                        self.ldq.issue(s, self.h1b[p][:, :],
                                       self.h1d[d][:, 1024 + tg * 512:1024 + (tg + 1) * 512]),
                        self.ldq.issue(s, self.wb1[p][:, :],
                                       self.wqT[d][:, hg * 512:(hg + 1) * 512]),
                    ]

                @b.tensor
                def _(t, tg=tg, d=d, p=p, it=it):
                    wait_toks(t, toks[it])
                    if d == 0:
                        t.wait_ge(vs.h, self._p2_kev if tg == 0 else self._p2_qev)
                    mm = None
                    for qf in range(4):
                        mm = t.matmul(psk[:, qf * 512:(qf + 1) * 512],
                                      self.wb1[p][:, qf * 128:(qf + 1) * 128],
                                      self.h1b[p][:, :], start=(d == 0), stop=(d == 15))
                    q_mm[it] = ts.inc(mm)
                    self.strm_last[p] = q_mm[it]

            last = q_mm[tg * 16 + 15]

            @b.vector
            def _(v, tg=tg, last=last, p3prev_ts=p3prev_ts):
                v.wait_ge(ts.h, max(last, p3prev_ts))
                el = None
                for qf in range(4):
                    dst = self.Qsb[:, qf * 1024 + tg * 512: qf * 1024 + (tg + 1) * 512]
                    a1 = vs.inc(v.tensor_scalar(dst, psk[:, qf * 512:(qf + 1) * 512],
                                                SC_HI, None, OP.mult))
                    v.wait_ge(vs.h, a1)
                    el = vs.inc(v.scalar_tensor_tensor(
                        dst, psk[:, qf * 512:(qf + 1) * 512], SC_LO, dst,
                        OP.mult, OP.add))
                self._p2_qev = el

    def p3_attn(self, hg):
        b = self.block
        ts, vs, ss = self.ts, self.vs, self.ss
        psB = self.psB
        s_ps = [psB[:, 0:128], psB[:, 512:640]]
        tr_ps = self.psA[:, 1024:1152]
        attn_ps = psB[:, 1024:1152]
        den_ps = psB[:, 1536:1537]
        o_ps = [self.psA[:, 0:512], self.psA[:, 512:1024]]
        tba = self.scrA
        pba = [self.scrB, self.scrC]
        q_done = self._p2_qev
        zone_s = self._p2_zone_s
        if not hasattr(self, "_p3_oadd"):
            self._p3_oadd = {}
            self._p3_afm_rd = [0, 0]
            self._p3_atm_rd = [0, 0]
            self._p3_tr_rd = 0
            self._p3_atm = 0
            self._p3_pb_rd = [0, 0]
        wo_tok = [None] * 4

        for hl in range(4):
            hp = hl % 2

            @b.sync
            def _(s, hg=hg, hl=hl, hp=hp):
                s.wait_ge(ts.h, self.wo_rd[hp])
                wo_tok[hl] = self.ldq.issue(s, self.wob[hp], self.woT[hg * 4 + hl])

            state = {}
            for it in range(10):
                if 1 <= it <= 8:
                    qtp = it - 1
                    stp = state[qtp]

                    @b.scalar
                    def _(s, qtp=qtp, stp=stp):
                        s.wait_ge(vs.h, stp["bias8"])
                        pb_par = qtp % 2
                        if self._p3_pb_rd[pb_par]:
                            s.wait_ge(ts.h, self._p3_pb_rd[pb_par])
                        last = None
                        for k in range(9):
                            last = ss.inc(s.activation(
                                pba[pb_par][:, k * 128:(k + 1) * 128],
                                tba[:, k * 128:(k + 1) * 128], AF.Exp,
                                scale=f32(CAP), bias=f32(-CAP)))
                        stp["exp8"] = last
                        self._p3_exp_last = last

                if it <= 7:
                    qt = it
                    st = state.setdefault(qt, {})
                    st["smm"] = [0] * 9
                    st["tanh"] = [0] * 9
                    ss_base = self.ss.n

                    @b.tensor
                    def _(t, qt=qt, hl=hl, st=st, ss_base=ss_base):
                        if qt == 0 and hl == 0:
                            t.wait_ge(vs.h, q_done)
                            t.wait_ge(ss.h, zone_s)
                        for k in range(9):
                            kt = 8 + qt - k
                            if k >= 2:
                                t.wait_ge(ss.h, ss_base + k - 1)
                            elif qt >= 1:
                                t.wait_ge(ss.h, state[qt - 1]["tanh"][8 - k])
                            st["smm"][k] = ts.inc(t.matmul(
                                s_ps[k % 2],
                                self.Ksb[:, hl * 2048 + kt * 128: hl * 2048 + (kt + 1) * 128],
                                self.Qsb[:, hl * 1024 + qt * 128: hl * 1024 + (qt + 1) * 128],
                                start=True, stop=True))

                    @b.scalar
                    def _(s, st=st):
                        if getattr(self, "_p3_exp_last", 0):
                            s.wait_ge(ss.h, self._p3_exp_last)
                        for k in range(9):
                            s.wait_ge(ts.h, st["smm"][k])
                            st["tanh"][k] = ss.inc(s.activation(
                                tba[:, k * 128:(k + 1) * 128], s_ps[k % 2], AF.Tanh))
                    assert st["tanh"][0] == ss_base + 1

                    @b.vector
                    def _(v, st=st):
                        bias = 0
                        for k in (0, 8):
                            v.wait_ge(ss.h, st["tanh"][k])
                            col = 0 if k == 0 else 128
                            bias = vs.inc(v.tensor_tensor(
                                tba[:, k * 128:(k + 1) * 128],
                                tba[:, k * 128:(k + 1) * 128],
                                self.c_bias[:, col:col + 128], OP.add))
                        st["bias8"] = bias

                if 1 <= it <= 8:
                    qtp = it - 1
                    stp = state[qtp]
                    pb_par = qtp % 2
                    pb = pba[pb_par]
                    last_pv = [0]

                    @b.tensor
                    def _(t, qtp=qtp, hl=hl, stp=stp, pb=pb):
                        t.wait_ge(ss.h, stp["exp8"])
                        t.wait_ge(vs.h, self._p3_atm)
                        for k in range(9):
                            kkt = 8 + qtp - k
                            t.matmul(attn_ps, pb[:, k * 128:(k + 1) * 128],
                                     self.Vsb[:, kkt * 512 + hl * 128: kkt * 512 + (hl + 1) * 128],
                                     start=(k == 0), stop=(k == 8))
                            last_pv[0] = ts.inc(t.matmul(
                                den_ps, pb[:, k * 128:(k + 1) * 128],
                                self.c_valid[:, qtp * 9 + k: qtp * 9 + k + 1],
                                start=(k == 0), stop=(k == 8)))
                    self._p3_pb_rd[pb_par] = last_pv[0]

                    @b.vector
                    def _(v, qtp=qtp, last_pv=last_pv):
                        v.wait_ge(ts.h, last_pv[0])
                        a = vs.inc(v.reciprocal(self.stats[:, 7:8], den_ps))
                        v.wait_ge(vs.h, a)
                        if self._p3_atm_rd[qtp % 2]:
                            v.wait_ge(ts.h, self._p3_atm_rd[qtp % 2])
                        self._p3_atm = vs.inc(v.tensor_scalar(
                            self.atm[qtp % 2][:], attn_ps, self.stats[:, 7:8],
                            None, OP.mult))
                    state[qtp]["atm"] = self._p3_atm

                if it >= 2:
                    qtq = it - 2
                    ap = qtq % 2

                    @b.tensor
                    def _(t, qtq=qtq, ap=ap):
                        t.wait_ge(vs.h, state[qtq]["atm"])
                        if self._p3_tr_rd:
                            t.wait_ge(vs.h, self._p3_tr_rd)
                        self._p3_tr = ts.inc(t.transpose(tr_ps, self.atm[ap][:],
                                                         self.c_ident[:]))
                    self._p3_atm_rd[ap] = self._p3_tr

                    trv = self._p3_tr
                    qp = qtq % 2

                    @b.vector
                    def _(v, trv=trv, qp=qp):
                        v.wait_ge(ts.h, trv)
                        if self._p3_afm_rd[qp]:
                            v.wait_ge(ts.h, self._p3_afm_rd[qp])
                        self._p3_afm = vs.inc(v.tensor_copy(self.afm[qp][:, :], tr_ps))
                    self._p3_tr_rd = self._p3_afm
                    afm_s = self._p3_afm

                    for j in range(4):
                        op = j % 2

                        @b.tensor
                        def _(t, qtq=qtq, hl=hl, j=j, op=op, qp=qp, afm_s=afm_s):
                            t.wait_ge(vs.h, afm_s)
                            wait_toks(t, [wo_tok[hl]])
                            prev = self._p3_oadd.get(op)
                            if prev is None:
                                t.wait_ge(vs.h, self._p2_qev)
                            else:
                                t.wait_ge(vs.h, prev)
                            mm = t.matmul(
                                o_ps[op],
                                self.afm[qp][:, :],
                                self.wob[hl % 2][:, j * 512:(j + 1) * 512],
                                start=True, stop=True)
                            self._p3_omm = ts.inc(mm)

                        omm = self._p3_omm

                        @b.vector
                        def _(v, qtq=qtq, hl=hl, hg=hg, j=j, op=op, omm=omm):
                            v.wait_ge(ts.h, omm)
                            dst = self.oacc[:, qtq * 2048 + j * 512: qtq * 2048 + (j + 1) * 512]
                            if hg == 0 and hl == 0:
                                self._p3_oadd[op] = vs.inc(v.tensor_copy(dst, o_ps[op]))
                            else:
                                self._p3_oadd[op] = vs.inc(
                                    v.tensor_tensor(dst, dst, o_ps[op], OP.add))
                    self._p3_afm_rd[qp] = self._p3_omm
            self.wo_rd[hp] = self._p3_omm

    def p4_x2(self):
        self.barrier()
        b = self.block
        vs, ss = self.vs, self.ss
        res_done = [0] * NTO
        st_toks = [None] * NTO
        for tt in range(NTO):
            xtok = [None]

            @b.sync
            def _(s, tt=tt):
                if tt >= 1:
                    s.wait_ge(vs.h, res_done[tt - 1])
                xtok[0] = self.ldq.issue(s, self.scrA[:, :], self.xall[8 + tt])

            @b.vector
            def _(v, tt=tt):
                wait_toks(v, [xtok[0]])
                if tt >= 1:
                    wait_toks(v, [st_toks[tt - 1]])

            @b.scalar
            def _(s, tt=tt):
                wait_toks(s, [xtok[0]])
                if tt >= 1:
                    s.wait_ge(vs.h, res_done[tt - 1])

            src = self.oacc[:, tt * 2048:(tt + 1) * 2048]
            self.emit_ln(src, self.scrC[:, :], resid=self.scrA[:, :])
            res_done[tt] = vs.n
            rd = vs.n

            @b.gpsimd
            def _(g, tt=tt, rd=rd):
                g.wait_ge(vs.h, rd)
                st_toks[tt] = self.sgq.issue(g, self.x2d[tt], self.scrC[:, :])

        self.barrier()

    def p5_ffn_half(self, hf):
        b = self.block
        ts, vs, ss = self.ts, self.vs, self.ss
        big = self.big
        w1gh = big[:, 0:2048].bitcast(BF16)
        w1gl = big[:, 2048:4096].bitcast(BF16)
        w3gh = big[:, 4096:6144].bitcast(BF16)
        w3gl = big[:, 6144:8192].bitcast(BF16)
        h2fh = big[:, 16384:20480].bitcast(BF16)
        h2fl = big[:, 20480:24576].bitcast(BF16)
        v2slot = [big[:, 8192:12288].bitcast(BF16), big[:, 12288:16384].bitcast(BF16)]
        v2h = [v2slot[0][:, 0:4096], v2slot[1][:, 0:4096]]
        v2l = [v2slot[0][:, 4096:8192], v2slot[1][:, 4096:8192]]
        w1s = self.oacc[:, 8192:12288]
        w3s = self.oacc[:, 12288:16384]
        sgb, hidb = self.sgb, self.hidb
        tfn32 = self.scrC[:, 0:512]
        tfnh = [self.scrC[:, 512:1024].bitcast(BF16)[:, 0:512],
                self.scrC[:, 1024:1536].bitcast(BF16)[:, 0:512]]
        tfnl = [self.scrC[:, 512:1024].bitcast(BF16)[:, 512:1024],
                self.scrC[:, 1024:1536].bitcast(BF16)[:, 512:1024]]
        g_ps = [self.psB[:, 0:512], self.psB[:, 1024:1536]]
        u_ps = [self.psB[:, 512:1024], self.psB[:, 1536:2048]]
        t_ps = u_ps
        out_ps = [self.psA[:, 0:1024], self.psA[:, 1024:2048]]

        ln_done = [0] * 4
        for tl in range(4):
            xtok = [None]

            @b.sync
            def _(s, tl=tl, hf=hf):
                if tl >= 1:
                    s.wait_ge(vs.h, ln_done[tl - 1])
                xtok[0] = self.ldq.issue(s, self.scrA[:, :], self.x2d[hf * 4 + tl])

            ts_now = ts.n

            @b.vector
            def _(v, tl=tl, ts_now=ts_now):
                wait_toks(v, [xtok[0]])
                if tl >= 1:
                    v.wait_ge(ts.h, ts_now)

            @b.scalar
            def _(s, tl=tl):
                wait_toks(s, [xtok[0]])
                if tl >= 1:
                    s.wait_ge(vs.h, ln_done[tl - 1])

            self.emit_ln(self.scrA[:, :], self.scrC[:, :])
            ln_done[tl] = vs.n
            lnv = vs.n

            @b.tensor
            def _(t, lnv=lnv):
                t.wait_ge(vs.h, lnv)

            self.emit_tm_to_fm(
                self.scrC,
                lambda i, tl=tl: h2fh[:, i * 512 + tl * 128: i * 512 + (tl + 1) * 128],
                split_to=lambda i, tl=tl: h2fl[:, i * 512 + tl * 128: i * 512 + (tl + 1) * 128])
            ln_done[tl] = self.vs.n
        h2_v, h2_s = self.fm_done_v, self.fm_done_s

        gu_done = {}
        w2_done = {}
        ev_done = {}
        t_evac = {}
        ld_w = {}
        ld_v2 = {}
        mul_done = {}
        split_v = {}
        split_s = {}

        def stage_loads(s, g):
            wt = []
            for d in range(16):
                wt.append(self.ldq.issue(s, w1s[:, d * 256:(d + 1) * 256],
                                         self.w1T[d][:, g * 256:(g + 1) * 256]))
                wt.append(self.ldq.issue(s, w3s[:, d * 256:(d + 1) * 256],
                                         self.w3T[d][:, g * 256:(g + 1) * 256]))
            ld_w[g] = wt

        def emit_split(g, guv):
            @b.scalar
            def _(s, g=g, guv=guv):
                wait_toks(s, ld_w[g])
                if guv:
                    s.wait_ge(ts.h, guv)
                last = None
                for d in range(16):
                    sl = slice(d * 256, (d + 1) * 256)
                    last = ss.inc(s.copy(w1gh[:, sl], w1s[:, sl]))
                    last = ss.inc(s.copy(w3gh[:, sl], w3s[:, sl]))
                split_s[g] = last

            @b.vector
            def _(v, g=g, guv=guv):
                wait_toks(v, ld_w[g])
                if guv:
                    v.wait_ge(ts.h, guv)
                v.wait_ge(ss.h, split_s[g])
                last = None
                for d in range(16):
                    sl = slice(d * 256, (d + 1) * 256)
                    last = vs.inc(v.tensor_tensor(w1gl[:, sl], w1s[:, sl],
                                                  w1gh[:, sl], OP.subtract))
                    last = vs.inc(v.tensor_tensor(w3gl[:, sl], w3s[:, sl],
                                                  w3gh[:, sl], OP.subtract))
                split_v[g] = last

        @b.sync
        def _(s, hf=hf):
            if hf == 1:
                s.wait_ge(ts.h, self._p5_half_ts)
                s.wait_ge(vs.h, self._p5_half_vs)
            stage_loads(s, 0)
        emit_split(0, self._p5_half_ts if hf == 1 else 0)

        for grp in range(NGRP):
            @b.sync
            def _(s, grp=grp, hf=hf):
                if grp + 1 < NGRP:
                    s.wait_ge(vs.h, split_v[grp])
                    stage_loads(s, grp + 1)
                if grp >= 2:
                    s.wait_ge(ts.h, w2_done[grp - 2])
                elif hf == 1:
                    s.wait_ge(ts.h, self._p5_half_ts)
                vt = []
                for j in range(GA):
                    vt.append(self.ldq.issue(s, v2h[grp % 2][:, j * 2048:(j + 1) * 2048],
                                             self.v2dh[grp * GA + j]))
                    vt.append(self.ldq.issue(s, v2l[grp % 2][:, j * 2048:(j + 1) * 2048],
                                             self.v2dl[grp * GA + j]))
                ld_v2[grp] = vt

            for al in range(GA):
                pk = grp * GA + al

                @b.tensor
                def _(t, grp=grp, al=al, pk=pk, hf=hf):
                    t.wait_ge(ss.h, split_s[grp])
                    t.wait_ge(vs.h, split_v[grp])
                    if grp == 0 and al == 0:
                        t.wait_ge(vs.h, h2_v)
                        t.wait_ge(ss.h, h2_s)
                    if pk >= 2:
                        t.wait_ge(vs.h, max(t_evac[pk - 2], mul_done[pk - 2]))
                    elif hf == 1:
                        t.wait_ge(vs.h, max(self._p5_tv_prev, self._p5_mul_prev))
                    mm = None
                    for ps_out, wh_, wl_ in ((g_ps[al], w1gh, w1gl),
                                             (u_ps[al], w3gh, w3gl)):
                        for d in range(16):
                            wsl = slice(d * 256 + al * 128, d * 256 + (al + 1) * 128)
                            hsl = slice(d * 512, (d + 1) * 512)
                            t.matmul(ps_out, wh_[:, wsl], h2fh[:, hsl],
                                     start=(d == 0), stop=False)
                            t.matmul(ps_out, wh_[:, wsl], h2fl[:, hsl],
                                     start=False, stop=False)
                            mm = t.matmul(ps_out, wl_[:, wsl], h2fh[:, hsl],
                                          start=False, stop=(d == 15))
                    gu_done[(grp, al)] = ts.inc(mm)
                guv = gu_done[(grp, al)]

                if al == GA - 1:
                    if grp + 1 < NGRP:
                        emit_split(grp + 1, guv)

                @b.scalar
                def _(s, al=al, guv=guv):
                    s.wait_ge(ts.h, guv)
                    self._p5_silu = ss.inc(s.activation(sgb[al], g_ps[al], AF.Silu))

                sil = self._p5_silu

                @b.vector
                def _(v, al=al, pk=pk, sil=sil):
                    v.wait_ge(ss.h, sil)
                    mul_done[pk] = vs.inc(v.tensor_tensor(hidb[al], sgb[al], u_ps[al], OP.mult))

                mulv = mul_done[pk]

                @b.tensor
                def _(t, al=al, mulv=mulv):
                    t.wait_ge(vs.h, mulv)
                    self._p5_tmm = ts.inc(t.matmul(t_ps[al], self.c_h128[:], hidb[al],
                                                   start=True, stop=True))

                tmm = self._p5_tmm

                @b.vector
                def _(v, grp=grp, al=al, pk=pk, tmm=tmm, hf=hf):
                    v.wait_ge(ts.h, tmm)
                    if grp >= 1:
                        v.wait_ge(ts.h, w2_done[grp - 1])
                    elif hf == 1:
                        v.wait_ge(ts.h, self._p5_half_ts)
                    a1 = vs.inc(v.tensor_scalar(tfn32, t_ps[al], FW_HI, None, OP.mult))
                    v.wait_ge(vs.h, a1)
                    a2 = vs.inc(v.scalar_tensor_tensor(
                        tfn32, t_ps[al], FW_LO, tfn32, OP.mult, OP.add))
                    v.wait_ge(vs.h, a2)
                    a3 = vs.inc(v.tensor_copy(tfnh[al], tfn32))
                    v.wait_ge(vs.h, a3)
                    t_evac[pk] = vs.inc(v.tensor_tensor(tfnl[al], tfn32, tfnh[al],
                                                        OP.subtract))

            te = max(t_evac[grp * GA], t_evac[grp * GA + 1])
            for idx in range(8):
                tl, dh = divmod(idx, 2)
                pp = idx % 2

                @b.tensor
                def _(t, grp=grp, idx=idx, tl=tl, dh=dh, pp=pp, te=te, hf=hf):
                    t.wait_ge(vs.h, te)
                    wait_toks(t, ld_v2[grp])
                    if idx >= 2:
                        t.wait_ge(vs.h, ev_done[(grp, idx - 2)])
                    elif grp >= 1:
                        t.wait_ge(vs.h, ev_done[(grp - 1, 6 + idx)])
                    elif hf == 1:
                        t.wait_ge(vs.h, self._p5_half_vs)
                    mm = None
                    for al in range(GA):
                        tsl = slice(tl * 128, (tl + 1) * 128)
                        for on in range(2):
                            vsl = slice(al * 2048 + dh * 1024 + on * 512,
                                        al * 2048 + dh * 1024 + (on + 1) * 512)
                            po = out_ps[pp][:, on * 512:(on + 1) * 512]
                            sl0 = grp % 2
                            t.matmul(po, tfnh[al][:, tsl], v2h[sl0][:, vsl],
                                     start=(al == 0), stop=False)
                            t.matmul(po, tfnh[al][:, tsl], v2l[sl0][:, vsl],
                                     start=False, stop=False)
                            mm = t.matmul(po, tfnl[al][:, tsl], v2h[sl0][:, vsl],
                                          start=False, stop=(al == GA - 1))
                    w2_done[grp] = ts.inc(mm)

                wd = w2_done[grp]

                @b.vector
                def _(v, grp=grp, idx=idx, tl=tl, dh=dh, pp=pp, wd=wd):
                    v.wait_ge(ts.h, wd)
                    dst = self.oacc[:, tl * 2048 + dh * 1024: tl * 2048 + (dh + 1) * 1024]
                    if grp == 0:
                        ev_done[(grp, idx)] = vs.inc(v.tensor_copy(dst, out_ps[pp]))
                    else:
                        ev_done[(grp, idx)] = vs.inc(v.tensor_tensor(dst, dst, out_ps[pp], OP.add))

        self._p5_half_ts = self.ts.n
        self._p5_half_vs = self.vs.n
        self._p5_tv_prev = t_evac[(NGRP - 1) * GA + GA - 1]
        self._p5_mul_prev = mul_done[(NGRP - 1) * GA + GA - 1]

        ffn_v = self.vs.n
        res_done = [0] * 4
        st_toks = [None] * 4
        for tl in range(4):
            xtok = [None]

            @b.sync
            def _(s, tl=tl, hf=hf, ffn_v=ffn_v):
                if tl >= 1:
                    s.wait_ge(vs.h, res_done[tl - 1])
                else:
                    s.wait_ge(vs.h, ffn_v)
                xtok[0] = self.ldq.issue(s, self.scrA[:, :], self.x2d[hf * 4 + tl])

            @b.vector
            def _(v, tl=tl, ffn_v=ffn_v):
                wait_toks(v, [xtok[0]])
                v.wait_ge(vs.h, ffn_v)
                if tl >= 1:
                    wait_toks(v, [st_toks[tl - 1]])

            @b.scalar
            def _(s, tl=tl, ffn_v=ffn_v):
                wait_toks(s, [xtok[0]])
                s.wait_ge(vs.h, ffn_v)
                if tl >= 1:
                    s.wait_ge(vs.h, res_done[tl - 1])

            src_t = self.oacc[:, tl * 2048:(tl + 1) * 2048]
            self.emit_ln(src_t, self.scrC[:, :], resid=self.scrA[:, :])
            res_done[tl] = vs.n
            rd = vs.n

            @b.gpsimd
            def _(g, tl=tl, hf=hf, rd=rd):
                g.wait_ge(vs.h, rd)
                st_toks[tl] = self.sgq.issue(g, self.y[hf * 4 + tl], self.scrC[:, :])

        self.barrier()

    def _finish(self):
        b = self.block
        vals = self.sgq.snapshot()

        @b.gpsimd
        def _(g):
            for h, n in vals:
                g.wait_ge(h, n)


def _hadamard(n):
    h = np.array([[1.0]], dtype=np.float64)
    while h.shape[0] < n:
        h = np.block([[h, h], [h, -h]])
    return h


def _build_consts():
    h64 = _hadamard(64).astype(f32)
    h128 = _hadamard(128).astype(f32)
    ident = np.eye(128, dtype=f32)
    kj = np.arange(128)[:, None]
    qi = np.arange(128)[None, :]
    tri = np.where(kj <= qi, 0.0, NEG).astype(f32)
    edge = np.where(kj > qi, 0.0, NEG).astype(f32)
    bias = np.ascontiguousarray(np.concatenate([tri, edge], axis=1))
    return h64, h128, ident, bias


def make_in_maps(x, wq, wk, wv, wo, w1, w2, w3):
    def t3(w, cols):
        return np.ascontiguousarray(np.ascontiguousarray(w.T).reshape(16, 128, cols))

    wqT, wkT, wvT, woT = t3(wq, D), t3(wk, D), t3(wv, D), t3(wo, D)
    w1T, w3T = t3(w1, HID), t3(w3, HID)
    w2r = np.ascontiguousarray(
        w2.reshape(D, 64, 128).transpose(1, 2, 0)).reshape(64, 128 * D)
    h64, h128, ident, bias = _build_consts()

    in_maps = []
    for core in range(8):
        bb, cc = divmod(core, 4)
        own = x[bb, cc * 1024:(cc + 1) * 1024]
        halo = (np.zeros((1024, D), f32) if cc == 0
                else x[bb, (cc - 1) * 1024: cc * 1024])
        xa = np.ascontiguousarray(
            np.concatenate([halo, own], axis=0)).reshape(NT, 128, D)
        valid = np.ones((128, 72), f32)
        if cc == 0:
            for qt in range(8):
                for kr in range(9):
                    if 8 + qt - kr < 8:
                        valid[:, qt * 9 + kr] = 0.0
        in_maps.append({
            "xall": xa,
            "wqT": wqT, "wkT": wkT, "wvT": wvT, "woT": woT,
            "w1T": w1T, "w3T": w3T, "w2r": w2r,
            "h64c": h64, "h128c": h128, "identc": ident,
            "biasc": bias, "validc": valid,
        })
    return in_maps


def kernel(L0_attn_x, L0_attn_Wq, L0_attn_Wk, L0_attn_Wv, L0_attn_Wo,
           L0_ffn_W1, L0_ffn_W2, L0_ffn_W3):
    x = np.asarray(L0_attn_x, dtype=f32)
    args = [np.asarray(a, dtype=f32) for a in
            (L0_attn_Wq, L0_attn_Wk, L0_attn_Wv, L0_attn_Wo,
             L0_ffn_W1, L0_ffn_W2, L0_ffn_W3)]
    wq, wk, wv, wo, w1, w2, w3 = args

    nc = Builder().build()
    in_maps = make_in_maps(x, wq, wk, wv, wo, w1, w2, w3)
    res = run_bass_kernel_spmd(nc, in_maps, core_ids=list(range(8)))
    out = np.empty((2, 4096, D), dtype=np.asarray(res.results[0]["y"]).dtype)
    for core in range(8):
        bb, cc = divmod(core, 4)
        out[bb, cc * 1024:(cc + 1) * 1024] = res.results[core]["y"].reshape(1024, D)
    return out


# version 17
# speedup vs baseline: 1.0879x; 1.0879x over previous
import numpy as np
import concourse.bass as bass
import concourse.mybir as mybir
from concourse.bass_utils import run_bass_kernel_spmd
from contextlib import ExitStack

F32 = mybir.dt.float32
BF16 = mybir.dt.bfloat16
AF = mybir.ActivationFunctionType
OP = mybir.AluOpType
AX = mybir.AxisListType
f32 = np.float32

D = 2048
HID = 8192
CAP = 30.0
SCALE = 1.0 / np.sqrt(128.0)
EPS = 1e-5
NT = 16
NTO = 8
NHG = 4
GA = 2
NGRP = 64 // GA
NEG = -100.0
SC_HI = float(f32(1.0 / (np.sqrt(128.0) * 30.0)))
SC_LO = float(f32(1.0 / (np.sqrt(np.float64(128.0)) * np.float64(30.0)) - np.float64(f32(1.0 / (np.sqrt(128.0) * 30.0)))))
FW_HI = float(f32(1.0 / np.sqrt(8192.0)))
FW_LO = float(f32(1.0 / np.sqrt(np.float64(8192.0)) - np.float64(f32(1.0 / np.sqrt(8192.0)))))


class Sem:
    def __init__(self, stack, nc, name):
        self.h = stack.enter_context(nc.semaphore(name))
        self.n = 0

    def inc(self, instr, k=1):
        instr.then_inc(self.h, k)
        self.n += k
        return self.n


class DmaQ:

    def __init__(self, stack, nc, name, n=12):
        self.lanes = [Sem(stack, nc, "%s%d" % (name, i)) for i in range(n)]
        self.i = 0

    def issue(self, eng, out, in_, **kw):
        ln = self.lanes[self.i % len(self.lanes)]
        self.i += 1
        if ln.n:
            eng.wait_ge(ln.h, ln.n)
        ln.inc(eng.dma_start(out=out, in_=in_, **kw), 16)
        return (ln.h, ln.n)

    def snapshot(self):
        return [(l.h, l.n) for l in self.lanes if l.n]


def wait_toks(eng, toks):
    best = {}
    for h, v in toks:
        key = id(h)
        if key not in best or best[key][1] < v:
            best[key] = (h, v)
    for h, v in best.values():
        eng.wait_ge(h, v)


class Builder:
    def __init__(self, phases=("p0", "p1", "p2p3", "p4", "p5")):
        self.nc = bass.Bass("TRN2", target_bir_lowering=False)
        self.stack = ExitStack()
        self.phases = phases

    def sb(self, name, shape):
        return self.stack.enter_context(self.nc.sbuf_tensor(name, shape, F32))

    def ps(self, name, shape):
        return self.stack.enter_context(self.nc.psum_tensor(name, shape, F32))

    def alias_bf16(self, name, base_handle, col_off, shape):
        addr = self.nc.lookup_mloc(base_handle).addr + col_off * 4
        return self.nc.alloc_sbuf_tensor_at(name, shape, BF16, offset=addr)

    def build(self):
        nc = self.nc
        dt = nc.dram_tensor
        self.xall = dt("xall", [NT, 128, D], F32, kind="ExternalInput").ap()
        self.wqT = dt("wqT", [16, 128, D], F32, kind="ExternalInput").ap()
        self.wkT = dt("wkT", [16, 128, D], F32, kind="ExternalInput").ap()
        self.wvT = dt("wvT", [16, 128, D], F32, kind="ExternalInput").ap()
        self.woT = dt("woT", [16, 128, D], F32, kind="ExternalInput").ap()
        self.w1T = dt("w1T", [16, 128, HID], F32, kind="ExternalInput").ap()
        self.w3T = dt("w3T", [16, 128, HID], F32, kind="ExternalInput").ap()
        self.w2r = dt("w2r", [64, 128 * D], F32, kind="ExternalInput").ap()
        self.h64c = dt("h64c", [64, 64], F32, kind="ExternalInput").ap()
        self.h128c = dt("h128c", [128, 128], F32, kind="ExternalInput").ap()
        self.identc = dt("identc", [128, 128], F32, kind="ExternalInput").ap()
        self.biasc = dt("biasc", [128, 256], F32, kind="ExternalInput").ap()
        self.validc = dt("validc", [128, 72], F32, kind="ExternalInput").ap()
        self.y = dt("y", [NTO, 128, D], F32, kind="ExternalOutput").ap()
        self.h1d = dt("h1d", [16, 128, D], F32, kind="Internal").ap()
        self.v2dh = dt("v2dh", [64, 128, D], BF16, kind="Internal").ap()
        self.v2dl = dt("v2dl", [64, 128, D], BF16, kind="Internal").ap()
        self.x2d = dt("x2d", [NTO, 128, D], F32, kind="Internal").ap()
        self.v2d = dt("v2d", [64, 128, D], F32, kind="Internal").ap()

        self.c_ident = self.sb("c_ident", [128, 128])
        self.c_h128 = self.sb("c_h128", [128, 128])
        self.c_h64 = self.sb("c_h64", [64, 64])
        self.c_h64b = self.sb("c_h64b", [128, 64])
        self.c_bias = self.sb("c_bias", [128, 256])
        self.c_valid = self.sb("c_valid", [128, 72])
        self.big = self.sb("big", [128, 24576])
        self.Ksb = self.big[:, 0:8192]
        self.Vsb = self.big[:, 8192:16384]
        self.Qsb = self.big[:, 16384:20480]
        self.wob = [self.big[:, 20480:22528], self.big[:, 22528:24576]]
        self.w1g = self.big[:, 0:4096]
        self.w3g = self.big[:, 4096:8192]
        self.v2g = [self.big[:, 8192:12288], self.big[:, 12288:16384]]
        self.h2f = self.big[:, 16384:24576]

        self.oacc = self.sb("oacc", [128, 16384])
        self.scrA = self.sb("scrA", [128, 2048])
        self.scrB = self.sb("scrB", [128, 2048])
        self.scrC = self.sb("scrC", [128, 2048])
        self.h1b = [self.sb("h1b%d" % i, [128, 512]) for i in range(3)]
        self.wb1 = [self.sb("wb1%d" % i, [128, 512]) for i in range(3)]
        self.wb2 = [self.sb("wb2%d" % i, [128, 512]) for i in range(3)]
        self.atm = [self.sb("atm0", [128, 128]), self.sb("atm1", [128, 128])]
        self.afm = [self.sb("afm0", [128, 128]), self.sb("afm1", [128, 128])]
        self.stats = self.sb("stats", [128, 8])
        self.sgb = [self.scrA[:, 0:512], self.scrA[:, 512:1024]]
        self.hidb = [self.scrA[:, 1024:1536], self.scrA[:, 1536:2048]]
        self.tfn = [self.scrC[:, 0:512], self.scrC[:, 512:1024]]

        self.psA = self.ps("psA", [128, 2048])
        self.psB = self.ps("psB", [128, 2048])

        st = self.stack
        self.ldq = DmaQ(st, nc, "ld", 12)
        self.sgq = DmaQ(st, nc, "sg", 8)
        self.ts = Sem(st, nc, "ts")
        self.vs = Sem(st, nc, "vs")
        self.ss = Sem(st, nc, "ss")
        self.gs = Sem(st, nc, "gs")

        self.strm_last = [0, 0, 0]
        self.wo_rd = [0, 0]

        self.block = st.enter_context(nc.Block())
        self._load_consts()
        if "p0" in self.phases:
            self.p0_v2()
        if "p1" in self.phases:
            self.p1_h1()
        if "p2p3" in self.phases:
            for hg in range(NHG):
                self.p2_kvq(hg)
                self.p3_attn(hg)
        if "p4" in self.phases:
            self.p4_x2()
        if "p5" in self.phases:
            for hf in range(2):
                self.p5_ffn_half(hf)
        self._finish()
        return nc

    def _load_consts(self):
        b = self.block
        toks = []

        @b.sync
        def _(s):
            toks.append(self.ldq.issue(s, self.c_ident[:], self.identc[:]))
            toks.append(self.ldq.issue(s, self.c_h128[:], self.h128c[:]))
            toks.append(self.ldq.issue(s, self.c_h64[:], self.h64c[:]))
            toks.append(self.ldq.issue(s, self.c_h64b[64:128, :], self.h64c[:]))
            toks.append(self.ldq.issue(s, self.c_bias[:], self.biasc[:]))
            toks.append(self.ldq.issue(s, self.c_valid[:], self.validc[:]))

        for deco in (b.tensor, b.vector, b.scalar, b.gpsimd):
            @deco
            def _(e):
                wait_toks(e, toks)


    def barrier(self):
        b = self.block
        vals = ([(s.h, s.n) for s in (self.ts, self.vs, self.ss, self.gs) if s.n]
                + self.ldq.snapshot() + self.sgq.snapshot())

        for deco in (b.sync, b.tensor, b.vector, b.scalar, b.gpsimd):
            @deco
            def _(e, vals=vals):
                for h, n in vals:
                    e.wait_ge(h, n)

    def p0_v2(self):
        b = self.block
        ts, vs, ss = self.ts, self.vs, self.ss
        w2b = [self.oacc[:, 0:2048], self.oacc[:, 2048:4096]]
        obh_t = [self.alias_bf16("p0h0", self.oacc, 4096, [128, 2048]),
                 self.alias_bf16("p0h1", self.oacc, 6144, [128, 2048])]
        obl_t = [self.alias_bf16("p0l0", self.oacc, 5120, [128, 2048]),
                 self.alias_bf16("p0l1", self.oacc, 7168, [128, 2048])]
        obh = [t[:, :] for t in obh_t]
        obl = [t[:, :] for t in obl_t]
        pso = [self.psA[0:64, :], self.psB[0:64, :]]
        tok_ld = [None] * 64
        tok_st = [None] * 64
        mm_sn = [0] * 64
        ev_sn = [0] * 64
        ev_sn2 = [0] * 64

        for i in range(64):
            p = i % 2
            ke, ko = 2 * i, 2 * i + 1

            @b.sync
            def _(s, i=i, p=p, ke=ke, ko=ko):
                if i >= 2:
                    s.wait_ge(ts.h, mm_sn[i - 2])
                tok_ld[i] = [
                    self.ldq.issue(s, w2b[p][0:64, :], self.w2r[:, ke * D:(ke + 1) * D]),
                    self.ldq.issue(s, w2b[p][64:128, :], self.w2r[:, ko * D:(ko + 1) * D]),
                ]

            @b.tensor
            def _(t, i=i, p=p):
                wait_toks(t, tok_ld[i])
                if i >= 1:
                    t.wait_ge(vs.h, ev_sn[i - 1])
                    t.wait_ge(ss.h, ev_sn2[i - 1])
                mm = None
                for j in range(4):
                    t.matmul(pso[0][:, j * 512:(j + 1) * 512],
                             self.c_h64[:, :],
                             w2b[p][0:64, j * 512:(j + 1) * 512],
                             start=True, stop=True)
                    mm = t.matmul(pso[1][:, j * 512:(j + 1) * 512],
                                  self.c_h64b[64:128, :],
                                  w2b[p][64:128, j * 512:(j + 1) * 512],
                                  start=True, stop=True)
                mm_sn[i] = ts.inc(mm)

            @b.scalar
            def _(s, i=i, p=p):
                s.wait_ge(ts.h, mm_sn[i])
                if i >= 2:
                    wait_toks(s, tok_st[i - 2])
                ss.inc(s.copy(obh[p][0:64, :], pso[0]))
                ev_sn2[i] = ss.inc(s.copy(obh[p][64:128, :], pso[1]))

            @b.vector
            def _(v, i=i, p=p):
                v.wait_ge(ts.h, mm_sn[i])
                v.wait_ge(ss.h, ev_sn2[i])
                if i >= 2:
                    wait_toks(v, tok_st[i - 2])
                vs.inc(v.tensor_tensor(obl[p][0:64, :], pso[0], obh[p][0:64, :],
                                       OP.subtract))
                ev_sn[i] = vs.inc(v.tensor_tensor(obl[p][64:128, :], pso[1],
                                                  obh[p][64:128, :], OP.subtract))

            @b.gpsimd
            def _(g, i=i, p=p, ke=ke, ko=ko):
                g.wait_ge(vs.h, ev_sn[i])
                tok_st[i] = [
                    self.sgq.issue(g, self.v2dh[:, ke, :], obh[p][0:64, :]),
                    self.sgq.issue(g, self.v2dh[:, ko, :], obh[p][64:128, :]),
                    self.sgq.issue(g, self.v2dl[:, ke, :], obl[p][0:64, :]),
                    self.sgq.issue(g, self.v2dl[:, ko, :], obl[p][64:128, :]),
                ]

        self.barrier()

    def emit_ln(self, src, dst, resid=None, n=D):
        b = self.block
        stx = self.stats
        vs, ss = self.vs, self.ss
        r = {}

        @b.vector
        def _(v):
            r["a"] = vs.inc(v.reduce_sum(stx[:, 0:1], src, axis=AX.X))

        @b.scalar
        def _(s):
            r["b"] = ss.inc(s.activation(self.scrB[:, 0:n], src, AF.Square,
                                         accum_out=stx[:, 1:2]))

        @b.vector
        def _(v):
            v.wait_ge(vs.h, r["a"])
            v.tensor_scalar(stx[:, 2:3], stx[:, 0:1], float(1.0 / n), None, OP.mult)
            v.wait_ge(ss.h, r["b"])
            c = vs.inc(v.tensor_scalar(stx[:, 3:4], stx[:, 1:2], float(1.0 / n), None, OP.mult))
            v.wait_ge(vs.h, c)
            d = vs.inc(v.tensor_tensor(stx[:, 6:7], stx[:, 2:3], stx[:, 2:3], OP.mult))
            v.wait_ge(vs.h, d)
            r["e"] = vs.inc(v.tensor_tensor(stx[:, 3:4], stx[:, 3:4], stx[:, 6:7], OP.subtract))

        @b.scalar
        def _(s):
            s.wait_ge(vs.h, r["e"])
            r["f"] = ss.inc(s.activation(stx[:, 4:5], stx[:, 3:4], AF.Sqrt, bias=f32(EPS)))

        @b.vector
        def _(v):
            v.wait_ge(ss.h, r["f"])
            g = vs.inc(v.reciprocal(stx[:, 4:5], stx[:, 4:5]))
            v.wait_ge(vs.h, g)
            h = vs.inc(v.tensor_tensor(stx[:, 5:6], stx[:, 2:3], stx[:, 4:5], OP.mult))
            v.wait_ge(vs.h, h)
            i = vs.inc(v.tensor_scalar(dst, src, stx[:, 4:5], stx[:, 5:6],
                                       OP.mult, OP.subtract))
            if resid is not None:
                v.wait_ge(vs.h, i)
                vs.inc(v.tensor_tensor(dst, dst, resid, OP.add))

    def emit_tm_to_fm(self, src, evac_to, split_to=None):
        b = self.block
        ts, vs, ss = self.ts, self.vs, self.ss
        trp = [self.psB[:, 0:128], self.psB[:, 512:640]]
        tsn = [0] * 16
        vs_base = self.vs.n
        ss_base = self.ss.n

        @b.tensor
        def _(t):
            for i in range(16):
                if i >= 2:
                    if split_to is not None:
                        t.wait_ge(vs.h, vs_base + 2 * (i - 2) + 2)
                    else:
                        j = (i - 2) // 2 + 1
                        if (i - 2) % 2 == 0:
                            t.wait_ge(vs.h, vs_base + j)
                        else:
                            t.wait_ge(ss.h, ss_base + j)
                tsn[i] = ts.inc(t.transpose(trp[i % 2], src[:, i * 128:(i + 1) * 128],
                                            self.c_ident[:]))

        if split_to is not None:
            @b.vector
            def _(v):
                for i in range(16):
                    v.wait_ge(ts.h, tsn[i])
                    a1 = vs.inc(v.tensor_copy(evac_to(i), trp[i % 2]))
                    v.wait_ge(vs.h, a1)
                    vs.inc(v.tensor_tensor(split_to(i), trp[i % 2], evac_to(i),
                                           OP.subtract))
            assert self.vs.n == vs_base + 32
        else:
            @b.vector
            def _(v):
                for i in range(0, 16, 2):
                    v.wait_ge(ts.h, tsn[i])
                    vs.inc(v.tensor_copy(evac_to(i), trp[0]))
            assert self.vs.n == vs_base + 8

            @b.scalar
            def _(s):
                for i in range(1, 16, 2):
                    s.wait_ge(ts.h, tsn[i])
                    ss.inc(s.copy(evac_to(i), trp[1]))
            assert self.ss.n == ss_base + 8
        self.fm_done_v = self.vs.n
        self.fm_done_s = self.ss.n

    def p1_h1(self):
        b = self.block
        ts, vs, ss = self.ts, self.vs, self.ss
        xb = [self.scrA, self.big[:, 0:2048]]
        ln_done = [0] * NT
        st_toks = [None] * NT
        for tt in range(NT):
            p = tt % 2
            xtok = [None]

            @b.sync
            def _(s, tt=tt, p=p):
                if tt >= 2:
                    s.wait_ge(vs.h, ln_done[tt - 2])
                xtok[0] = self.ldq.issue(s, xb[p][:, :], self.xall[tt])

            ts_now = ts.n

            @b.vector
            def _(v, tt=tt, ts_now=ts_now):
                wait_toks(v, [xtok[0]])
                if tt >= 1:
                    v.wait_ge(ts.h, ts_now)

            @b.scalar
            def _(s, tt=tt):
                wait_toks(s, [xtok[0]])
                if tt >= 1:
                    wait_toks(s, st_toks[tt - 1])

            self.emit_ln(xb[p][:, :], self.scrC[:, :])
            ln_done[tt] = vs.n
            ln_v = vs.n

            @b.tensor
            def _(t, ln_v=ln_v):
                t.wait_ge(vs.h, ln_v)

            self.emit_tm_to_fm(self.scrC, lambda i: self.scrB[:, i * 128:(i + 1) * 128])
            fmv, fms = self.fm_done_v, self.fm_done_s

            @b.gpsimd
            def _(g, tt=tt, fmv=fmv, fms=fms):
                g.wait_ge(vs.h, fmv)
                g.wait_ge(ss.h, fms)
                tl = []
                for dc in range(16):
                    tl.append(self.sgq.issue(
                        g, self.h1d[dc][:, tt * 128:(tt + 1) * 128],
                        self.scrB[:, dc * 128:(dc + 1) * 128]))
                st_toks[tt] = tl

            @b.vector
            def _(v, tt=tt):
                wait_toks(v, st_toks[tt])

        self.barrier()

    def p2_kvq(self, hg):
        b = self.block
        ts, vs, ss = self.ts, self.vs, self.ss
        psk, psv = self.psA, self.psB
        p3prev_ts = self.ts.n
        self._p2_zone_v = self.vs.n
        self._p2_zone_s = self.ss.n
        mm_done = {}
        toks = {}

        for tg in range(4):
            for d in range(16):
                p = (tg * 16 + d) % 3
                it = tg * 16 + d

                @b.sync
                def _(s, hg=hg, tg=tg, d=d, p=p, it=it):
                    s.wait_ge(ts.h, self.strm_last[p])
                    toks[it] = [
                        self.ldq.issue(s, self.h1b[p][:, :],
                                       self.h1d[d][:, tg * 512:(tg + 1) * 512]),
                        self.ldq.issue(s, self.wb1[p][:, :],
                                       self.wkT[d][:, hg * 512:(hg + 1) * 512]),
                        self.ldq.issue(s, self.wb2[p][:, :],
                                       self.wvT[d][:, hg * 512:(hg + 1) * 512]),
                    ]

                @b.tensor
                def _(t, d=d, p=p, it=it):
                    wait_toks(t, toks[it])
                    if d == 0:
                        t.wait_ge(vs.h, self._p2_zone_v)
                        t.wait_ge(ss.h, self._p2_zone_s)
                    mm = None
                    for kf in range(4):
                        t.matmul(psk[:, kf * 512:(kf + 1) * 512],
                                 self.wb1[p][:, kf * 128:(kf + 1) * 128],
                                 self.h1b[p][:, :], start=(d == 0), stop=(d == 15))
                    for vf in range(4):
                        mm = t.matmul(psv[:, vf * 512:(vf + 1) * 512],
                                      self.wb2[p][:, vf * 128:(vf + 1) * 128],
                                      self.h1b[p][:, :], start=(d == 0), stop=(d == 15))
                    mm_done[it] = ts.inc(mm)
                    self.strm_last[p] = mm_done[it]

            kv_mm = mm_done[tg * 16 + 15]

            @b.vector
            def _(v, tg=tg, kv_mm=kv_mm, p3prev_ts=p3prev_ts):
                v.wait_ge(ts.h, max(kv_mm, p3prev_ts))
                last = None
                for kf in range(4):
                    last = vs.inc(v.tensor_copy(
                        self.Ksb[:, kf * 2048 + tg * 512: kf * 2048 + (tg + 1) * 512],
                        psk[:, kf * 512:(kf + 1) * 512]))
                self._p2_kev = last

            @b.scalar
            def _(s, tg=tg, kv_mm=kv_mm, p3prev_ts=p3prev_ts):
                s.wait_ge(ts.h, max(kv_mm, p3prev_ts))
                last = None
                for vf in range(4):
                    last = ss.inc(s.copy(self.scrA[:, vf * 512:(vf + 1) * 512],
                                         psv[:, vf * 512:(vf + 1) * 512]))
                self._p2_vev = last

            vt_done = self._p2_vev
            trp = [psv[:, 0:128], psv[:, 512:640]]
            tsn = [0] * 16
            evn = [0] * 16
            eve = [None] * 16
            for i in range(16):
                vf, sub = divmod(i, 4)
                kt = tg * 4 + sub
                p = i % 2

                @b.tensor
                def _(t, i=i, p=p, vf=vf, vt_done=vt_done):
                    if i == 0:
                        t.wait_ge(ss.h, vt_done)
                    if i >= 2:
                        t.wait_ge(eve[i - 2].h, evn[i - 2])
                    tsn[i] = ts.inc(t.transpose(
                        trp[p],
                        self.scrA[:, vf * 512 + sub * 128: vf * 512 + (sub + 1) * 128],
                        self.c_ident[:]))

                dst = self.Vsb[:, kt * 512 + vf * 128: kt * 512 + (vf + 1) * 128]
                if i % 4 < 2:
                    @b.vector
                    def _(v, i=i, p=p, dst=dst):
                        v.wait_ge(ts.h, tsn[i])
                        evn[i] = vs.inc(v.tensor_copy(dst, trp[p]))
                    eve[i] = vs
                else:
                    @b.scalar
                    def _(s, i=i, p=p, dst=dst):
                        s.wait_ge(ts.h, tsn[i])
                        evn[i] = ss.inc(s.copy(dst, trp[p]))
                    eve[i] = ss
            self._p2_zone_v = self.vs.n
            self._p2_zone_s = self.ss.n

        q_mm = {}
        for tg in range(2):
            for d in range(16):
                p = (tg * 16 + d) % 3
                it = tg * 16 + d

                @b.sync
                def _(s, hg=hg, tg=tg, d=d, p=p, it=it):
                    s.wait_ge(ts.h, self.strm_last[p])
                    toks[it] = [
                        self.ldq.issue(s, self.h1b[p][:, :],
                                       self.h1d[d][:, 1024 + tg * 512:1024 + (tg + 1) * 512]),
                        self.ldq.issue(s, self.wb1[p][:, :],
                                       self.wqT[d][:, hg * 512:(hg + 1) * 512]),
                    ]

                @b.tensor
                def _(t, tg=tg, d=d, p=p, it=it):
                    wait_toks(t, toks[it])
                    if d == 0:
                        t.wait_ge(vs.h, self._p2_kev if tg == 0 else self._p2_qev)
                    mm = None
                    for qf in range(4):
                        mm = t.matmul(psk[:, qf * 512:(qf + 1) * 512],
                                      self.wb1[p][:, qf * 128:(qf + 1) * 128],
                                      self.h1b[p][:, :], start=(d == 0), stop=(d == 15))
                    q_mm[it] = ts.inc(mm)
                    self.strm_last[p] = q_mm[it]

            last = q_mm[tg * 16 + 15]

            @b.vector
            def _(v, tg=tg, last=last, p3prev_ts=p3prev_ts):
                v.wait_ge(ts.h, max(last, p3prev_ts))
                el = None
                for qf in range(4):
                    dst = self.Qsb[:, qf * 1024 + tg * 512: qf * 1024 + (tg + 1) * 512]
                    a1 = vs.inc(v.tensor_scalar(dst, psk[:, qf * 512:(qf + 1) * 512],
                                                SC_HI, None, OP.mult))
                    v.wait_ge(vs.h, a1)
                    el = vs.inc(v.scalar_tensor_tensor(
                        dst, psk[:, qf * 512:(qf + 1) * 512], SC_LO, dst,
                        OP.mult, OP.add))
                self._p2_qev = el

    def p3_attn(self, hg):
        b = self.block
        ts, vs, ss = self.ts, self.vs, self.ss
        psB = self.psB
        s_ps = [psB[:, 0:128], psB[:, 512:640]]
        tr_ps = self.psA[:, 1024:1152]
        attn_ps = psB[:, 1024:1152]
        den_ps = psB[:, 1536:1537]
        o_ps = [self.psA[:, 0:512], self.psA[:, 512:1024]]
        tba = self.scrA
        pba = [self.scrB, self.scrC]
        q_done = self._p2_qev
        zone_s = self._p2_zone_s
        if not hasattr(self, "_p3_oadd"):
            self._p3_oadd = {}
            self._p3_afm_rd = [0, 0]
            self._p3_atm_rd = [0, 0]
            self._p3_tr_rd = 0
            self._p3_atm = 0
            self._p3_pb_rd = [0, 0]
        wo_tok = [None] * 4

        for hl in range(4):
            hp = hl % 2

            @b.sync
            def _(s, hg=hg, hl=hl, hp=hp):
                s.wait_ge(ts.h, self.wo_rd[hp])
                wo_tok[hl] = self.ldq.issue(s, self.wob[hp], self.woT[hg * 4 + hl])

            state = {}
            for it in range(10):
                if 1 <= it <= 8:
                    qtp = it - 1
                    stp = state[qtp]

                    @b.scalar
                    def _(s, qtp=qtp, stp=stp):
                        s.wait_ge(vs.h, stp["bias8"])
                        pb_par = qtp % 2
                        if self._p3_pb_rd[pb_par]:
                            s.wait_ge(ts.h, self._p3_pb_rd[pb_par])
                        last = None
                        for k in range(9):
                            last = ss.inc(s.activation(
                                pba[pb_par][:, k * 128:(k + 1) * 128],
                                tba[:, k * 128:(k + 1) * 128], AF.Exp,
                                scale=f32(CAP), bias=f32(-CAP)))
                        stp["exp8"] = last
                        self._p3_exp_last = last

                if it <= 7:
                    qt = it
                    st = state.setdefault(qt, {})
                    st["smm"] = [0] * 9
                    st["tanh"] = [0] * 9
                    ss_base = self.ss.n

                    @b.tensor
                    def _(t, qt=qt, hl=hl, st=st, ss_base=ss_base):
                        if qt == 0 and hl == 0:
                            t.wait_ge(vs.h, q_done)
                            t.wait_ge(ss.h, zone_s)
                        for k in range(9):
                            kt = 8 + qt - k
                            if k >= 2:
                                t.wait_ge(ss.h, ss_base + k - 1)
                            elif qt >= 1:
                                t.wait_ge(ss.h, state[qt - 1]["tanh"][8 - k])
                            st["smm"][k] = ts.inc(t.matmul(
                                s_ps[k % 2],
                                self.Ksb[:, hl * 2048 + kt * 128: hl * 2048 + (kt + 1) * 128],
                                self.Qsb[:, hl * 1024 + qt * 128: hl * 1024 + (qt + 1) * 128],
                                start=True, stop=True))

                    @b.scalar
                    def _(s, st=st):
                        if getattr(self, "_p3_exp_last", 0):
                            s.wait_ge(ss.h, self._p3_exp_last)
                        for k in range(9):
                            s.wait_ge(ts.h, st["smm"][k])
                            st["tanh"][k] = ss.inc(s.activation(
                                tba[:, k * 128:(k + 1) * 128], s_ps[k % 2], AF.Tanh))
                    assert st["tanh"][0] == ss_base + 1

                    @b.vector
                    def _(v, st=st):
                        bias = 0
                        for k in (0, 8):
                            v.wait_ge(ss.h, st["tanh"][k])
                            col = 0 if k == 0 else 128
                            bias = vs.inc(v.tensor_tensor(
                                tba[:, k * 128:(k + 1) * 128],
                                tba[:, k * 128:(k + 1) * 128],
                                self.c_bias[:, col:col + 128], OP.add))
                        st["bias8"] = bias

                if 1 <= it <= 8:
                    qtp = it - 1
                    stp = state[qtp]
                    pb_par = qtp % 2
                    pb = pba[pb_par]
                    last_pv = [0]

                    @b.tensor
                    def _(t, qtp=qtp, hl=hl, stp=stp, pb=pb):
                        t.wait_ge(ss.h, stp["exp8"])
                        t.wait_ge(vs.h, self._p3_atm)
                        for k in range(9):
                            kkt = 8 + qtp - k
                            t.matmul(attn_ps, pb[:, k * 128:(k + 1) * 128],
                                     self.Vsb[:, kkt * 512 + hl * 128: kkt * 512 + (hl + 1) * 128],
                                     start=(k == 0), stop=(k == 8))
                            last_pv[0] = ts.inc(t.matmul(
                                den_ps, pb[:, k * 128:(k + 1) * 128],
                                self.c_valid[:, qtp * 9 + k: qtp * 9 + k + 1],
                                start=(k == 0), stop=(k == 8)))
                    self._p3_pb_rd[pb_par] = last_pv[0]

                    @b.vector
                    def _(v, qtp=qtp, last_pv=last_pv):
                        v.wait_ge(ts.h, last_pv[0])
                        a = vs.inc(v.reciprocal(self.stats[:, 7:8], den_ps))
                        v.wait_ge(vs.h, a)
                        if self._p3_atm_rd[qtp % 2]:
                            v.wait_ge(ts.h, self._p3_atm_rd[qtp % 2])
                        self._p3_atm = vs.inc(v.tensor_scalar(
                            self.atm[qtp % 2][:], attn_ps, self.stats[:, 7:8],
                            None, OP.mult))
                    state[qtp]["atm"] = self._p3_atm

                if it >= 2:
                    qtq = it - 2
                    ap = qtq % 2

                    @b.tensor
                    def _(t, qtq=qtq, ap=ap):
                        t.wait_ge(vs.h, state[qtq]["atm"])
                        if self._p3_tr_rd:
                            t.wait_ge(vs.h, self._p3_tr_rd)
                        self._p3_tr = ts.inc(t.transpose(tr_ps, self.atm[ap][:],
                                                         self.c_ident[:]))
                    self._p3_atm_rd[ap] = self._p3_tr

                    trv = self._p3_tr
                    qp = qtq % 2

                    @b.vector
                    def _(v, trv=trv, qp=qp):
                        v.wait_ge(ts.h, trv)
                        if self._p3_afm_rd[qp]:
                            v.wait_ge(ts.h, self._p3_afm_rd[qp])
                        self._p3_afm = vs.inc(v.tensor_copy(self.afm[qp][:, :], tr_ps))
                    self._p3_tr_rd = self._p3_afm
                    afm_s = self._p3_afm

                    for j in range(4):
                        op = j % 2

                        @b.tensor
                        def _(t, qtq=qtq, hl=hl, j=j, op=op, qp=qp, afm_s=afm_s):
                            t.wait_ge(vs.h, afm_s)
                            wait_toks(t, [wo_tok[hl]])
                            prev = self._p3_oadd.get(op)
                            if prev is None:
                                t.wait_ge(vs.h, self._p2_qev)
                            else:
                                t.wait_ge(vs.h, prev)
                            mm = t.matmul(
                                o_ps[op],
                                self.afm[qp][:, :],
                                self.wob[hl % 2][:, j * 512:(j + 1) * 512],
                                start=True, stop=True)
                            self._p3_omm = ts.inc(mm)

                        omm = self._p3_omm

                        @b.vector
                        def _(v, qtq=qtq, hl=hl, hg=hg, j=j, op=op, omm=omm):
                            v.wait_ge(ts.h, omm)
                            dst = self.oacc[:, qtq * 2048 + j * 512: qtq * 2048 + (j + 1) * 512]
                            if hg == 0 and hl == 0:
                                self._p3_oadd[op] = vs.inc(v.tensor_copy(dst, o_ps[op]))
                            else:
                                self._p3_oadd[op] = vs.inc(
                                    v.tensor_tensor(dst, dst, o_ps[op], OP.add))
                    self._p3_afm_rd[qp] = self._p3_omm
            self.wo_rd[hp] = self._p3_omm

    def p4_x2(self):
        self.barrier()
        b = self.block
        vs, ss = self.vs, self.ss
        res_done = [0] * NTO
        st_toks = [None] * NTO
        for tt in range(NTO):
            xtok = [None]

            @b.sync
            def _(s, tt=tt):
                if tt >= 1:
                    s.wait_ge(vs.h, res_done[tt - 1])
                xtok[0] = self.ldq.issue(s, self.scrA[:, :], self.xall[8 + tt])

            @b.vector
            def _(v, tt=tt):
                wait_toks(v, [xtok[0]])
                if tt >= 1:
                    wait_toks(v, [st_toks[tt - 1]])

            @b.scalar
            def _(s, tt=tt):
                wait_toks(s, [xtok[0]])
                if tt >= 1:
                    s.wait_ge(vs.h, res_done[tt - 1])

            src = self.oacc[:, tt * 2048:(tt + 1) * 2048]
            self.emit_ln(src, self.scrC[:, :], resid=self.scrA[:, :])
            res_done[tt] = vs.n
            rd = vs.n

            @b.gpsimd
            def _(g, tt=tt, rd=rd):
                g.wait_ge(vs.h, rd)
                st_toks[tt] = self.sgq.issue(g, self.x2d[tt], self.scrC[:, :])

        self.barrier()

    def p5_ffn_half(self, hf):
        b = self.block
        ts, vs, ss = self.ts, self.vs, self.ss
        big = self.big
        if not hasattr(self, "_p5_alias"):
            A = self.alias_bf16
            self._p5_alias = dict(
                w1gh=A("w1gh", big, 0, [128, 4096]),
                w1gl=A("w1gl", big, 2048, [128, 4096]),
                w3gh=A("w3gh", big, 4096, [128, 4096]),
                w3gl=A("w3gl", big, 6144, [128, 4096]),
                h2fh=A("h2fh", big, 16384, [128, 8192]),
                h2fl=A("h2fl", big, 20480, [128, 8192]),
                v2s0=A("v2s0", big, 8192, [128, 8192]),
                v2s1=A("v2s1", big, 12288, [128, 8192]),
                tf0=A("tf0", self.scrC, 512, [128, 1024]),
                tf1=A("tf1", self.scrC, 1024, [128, 1024]),
            )
        AL = self._p5_alias
        w1gh = AL["w1gh"][:, :]
        w1gl = AL["w1gl"][:, :]
        w3gh = AL["w3gh"][:, :]
        w3gl = AL["w3gl"][:, :]
        h2fh = AL["h2fh"][:, :]
        h2fl = AL["h2fl"][:, :]
        v2slot = [AL["v2s0"][:, :], AL["v2s1"][:, :]]
        v2h = [v2slot[0][:, 0:4096], v2slot[1][:, 0:4096]]
        v2l = [v2slot[0][:, 4096:8192], v2slot[1][:, 4096:8192]]
        w1s = self.oacc[:, 8192:12288]
        w3s = self.oacc[:, 12288:16384]
        sgb, hidb = self.sgb, self.hidb
        tfn32 = self.scrC[:, 0:512]
        tfnh = [AL["tf0"][:, 0:512], AL["tf1"][:, 0:512]]
        tfnl = [AL["tf0"][:, 512:1024], AL["tf1"][:, 512:1024]]
        g_ps = [self.psB[:, 0:512], self.psB[:, 1024:1536]]
        u_ps = [self.psB[:, 512:1024], self.psB[:, 1536:2048]]
        t_ps = u_ps
        out_ps = [self.psA[:, 0:1024], self.psA[:, 1024:2048]]

        ln_done = [0] * 4
        for tl in range(4):
            xtok = [None]

            @b.sync
            def _(s, tl=tl, hf=hf):
                if tl >= 1:
                    s.wait_ge(vs.h, ln_done[tl - 1])
                xtok[0] = self.ldq.issue(s, self.scrA[:, :], self.x2d[hf * 4 + tl])

            ts_now = ts.n

            @b.vector
            def _(v, tl=tl, ts_now=ts_now):
                wait_toks(v, [xtok[0]])
                if tl >= 1:
                    v.wait_ge(ts.h, ts_now)

            @b.scalar
            def _(s, tl=tl):
                wait_toks(s, [xtok[0]])
                if tl >= 1:
                    s.wait_ge(vs.h, ln_done[tl - 1])

            self.emit_ln(self.scrA[:, :], self.scrC[:, :])
            ln_done[tl] = vs.n
            lnv = vs.n

            @b.tensor
            def _(t, lnv=lnv):
                t.wait_ge(vs.h, lnv)

            self.emit_tm_to_fm(
                self.scrC,
                lambda i, tl=tl: h2fh[:, i * 512 + tl * 128: i * 512 + (tl + 1) * 128],
                split_to=lambda i, tl=tl: h2fl[:, i * 512 + tl * 128: i * 512 + (tl + 1) * 128])
            ln_done[tl] = self.vs.n
        h2_v, h2_s = self.fm_done_v, self.fm_done_s

        gu_done = {}
        w2_done = {}
        ev_done = {}
        t_evac = {}
        ld_w = {}
        ld_v2 = {}
        mul_done = {}
        split_v = {}
        split_s = {}

        def stage_loads(s, g):
            wt = []
            for d in range(16):
                wt.append(self.ldq.issue(s, w1s[:, d * 256:(d + 1) * 256],
                                         self.w1T[d][:, g * 256:(g + 1) * 256]))
                wt.append(self.ldq.issue(s, w3s[:, d * 256:(d + 1) * 256],
                                         self.w3T[d][:, g * 256:(g + 1) * 256]))
            ld_w[g] = wt

        def emit_split(g, guv):
            @b.scalar
            def _(s, g=g, guv=guv):
                wait_toks(s, ld_w[g])
                if guv:
                    s.wait_ge(ts.h, guv)
                last = None
                for d in range(16):
                    sl = slice(d * 256, (d + 1) * 256)
                    last = ss.inc(s.copy(w1gh[:, sl], w1s[:, sl]))
                    last = ss.inc(s.copy(w3gh[:, sl], w3s[:, sl]))
                split_s[g] = last

            @b.gpsimd
            def _(g_, g=g, guv=guv):
                wait_toks(g_, ld_w[g])
                if guv:
                    g_.wait_ge(ts.h, guv)
                g_.wait_ge(ss.h, split_s[g])
                last = None
                gs = self.gs
                for d in range(16):
                    sl = slice(d * 256, (d + 1) * 256)
                    last = gs.inc(g_.tensor_tensor(w1gl[:, sl], w1s[:, sl],
                                                   w1gh[:, sl], OP.subtract))
                    last = gs.inc(g_.tensor_tensor(w3gl[:, sl], w3s[:, sl],
                                                   w3gh[:, sl], OP.subtract))
                split_v[g] = last

        @b.sync
        def _(s, hf=hf):
            if hf == 1:
                s.wait_ge(ts.h, self._p5_half_ts)
                s.wait_ge(vs.h, self._p5_half_vs)
            stage_loads(s, 0)
        emit_split(0, self._p5_half_ts if hf == 1 else 0)

        for grp in range(NGRP):
            @b.sync
            def _(s, grp=grp, hf=hf):
                if grp + 1 < NGRP:
                    s.wait_ge(self.gs.h, split_v[grp])
                    stage_loads(s, grp + 1)
                if grp >= 2:
                    s.wait_ge(ts.h, w2_done[grp - 2])
                elif hf == 1:
                    s.wait_ge(ts.h, self._p5_half_ts)
                vt = []
                for j in range(GA):
                    vt.append(self.ldq.issue(s, v2h[grp % 2][:, j * 2048:(j + 1) * 2048],
                                             self.v2dh[grp * GA + j]))
                    vt.append(self.ldq.issue(s, v2l[grp % 2][:, j * 2048:(j + 1) * 2048],
                                             self.v2dl[grp * GA + j]))
                ld_v2[grp] = vt

            for al in range(GA):
                pk = grp * GA + al

                @b.tensor
                def _(t, grp=grp, al=al, pk=pk, hf=hf):
                    t.wait_ge(ss.h, split_s[grp])
                    t.wait_ge(self.gs.h, split_v[grp])
                    if grp == 0 and al == 0:
                        t.wait_ge(vs.h, h2_v)
                        t.wait_ge(ss.h, h2_s)
                    if pk >= 2:
                        t.wait_ge(vs.h, max(t_evac[pk - 2], mul_done[pk - 2]))
                    elif hf == 1:
                        t.wait_ge(vs.h, max(self._p5_tv_prev, self._p5_mul_prev))
                    mm = None
                    for ps_out, wh_, wl_ in ((g_ps[al], w1gh, w1gl),
                                             (u_ps[al], w3gh, w3gl)):
                        for d in range(16):
                            wsl = slice(d * 256 + al * 128, d * 256 + (al + 1) * 128)
                            hsl = slice(d * 512, (d + 1) * 512)
                            t.matmul(ps_out, wh_[:, wsl], h2fh[:, hsl],
                                     start=(d == 0), stop=False)
                            t.matmul(ps_out, wh_[:, wsl], h2fl[:, hsl],
                                     start=False, stop=False)
                            mm = t.matmul(ps_out, wl_[:, wsl], h2fh[:, hsl],
                                          start=False, stop=(d == 15))
                    gu_done[(grp, al)] = ts.inc(mm)
                guv = gu_done[(grp, al)]

                if al == GA - 1:
                    if grp + 1 < NGRP:
                        emit_split(grp + 1, guv)

                @b.scalar
                def _(s, al=al, guv=guv):
                    s.wait_ge(ts.h, guv)
                    self._p5_silu = ss.inc(s.activation(sgb[al], g_ps[al], AF.Silu))

                sil = self._p5_silu

                @b.vector
                def _(v, al=al, pk=pk, sil=sil):
                    v.wait_ge(ss.h, sil)
                    mul_done[pk] = vs.inc(v.tensor_tensor(hidb[al], sgb[al], u_ps[al], OP.mult))

                mulv = mul_done[pk]

                @b.tensor
                def _(t, al=al, mulv=mulv):
                    t.wait_ge(vs.h, mulv)
                    self._p5_tmm = ts.inc(t.matmul(t_ps[al], self.c_h128[:], hidb[al],
                                                   start=True, stop=True))

                tmm = self._p5_tmm

                @b.vector
                def _(v, grp=grp, al=al, pk=pk, tmm=tmm, hf=hf):
                    v.wait_ge(ts.h, tmm)
                    if grp >= 1:
                        v.wait_ge(ts.h, w2_done[grp - 1])
                    elif hf == 1:
                        v.wait_ge(ts.h, self._p5_half_ts)
                    a1 = vs.inc(v.tensor_scalar(tfn32, t_ps[al], FW_HI, None, OP.mult))
                    v.wait_ge(vs.h, a1)
                    a2 = vs.inc(v.scalar_tensor_tensor(
                        tfn32, t_ps[al], FW_LO, tfn32, OP.mult, OP.add))
                    v.wait_ge(vs.h, a2)
                    a3 = vs.inc(v.tensor_copy(tfnh[al], tfn32))
                    v.wait_ge(vs.h, a3)
                    t_evac[pk] = vs.inc(v.tensor_tensor(tfnl[al], tfn32, tfnh[al],
                                                        OP.subtract))

            te = max(t_evac[grp * GA], t_evac[grp * GA + 1])
            for idx in range(8):
                tl, dh = divmod(idx, 2)
                pp = idx % 2

                @b.tensor
                def _(t, grp=grp, idx=idx, tl=tl, dh=dh, pp=pp, te=te, hf=hf):
                    t.wait_ge(vs.h, te)
                    wait_toks(t, ld_v2[grp])
                    if idx >= 2:
                        t.wait_ge(vs.h, ev_done[(grp, idx - 2)])
                    elif grp >= 1:
                        t.wait_ge(vs.h, ev_done[(grp - 1, 6 + idx)])
                    elif hf == 1:
                        t.wait_ge(vs.h, self._p5_half_vs)
                    mm = None
                    for al in range(GA):
                        tsl = slice(tl * 128, (tl + 1) * 128)
                        for on in range(2):
                            vsl = slice(al * 2048 + dh * 1024 + on * 512,
                                        al * 2048 + dh * 1024 + (on + 1) * 512)
                            po = out_ps[pp][:, on * 512:(on + 1) * 512]
                            sl0 = grp % 2
                            t.matmul(po, tfnh[al][:, tsl], v2h[sl0][:, vsl],
                                     start=(al == 0), stop=False)
                            t.matmul(po, tfnh[al][:, tsl], v2l[sl0][:, vsl],
                                     start=False, stop=False)
                            mm = t.matmul(po, tfnl[al][:, tsl], v2h[sl0][:, vsl],
                                          start=False, stop=(al == GA - 1))
                    w2_done[grp] = ts.inc(mm)

                wd = w2_done[grp]

                @b.vector
                def _(v, grp=grp, idx=idx, tl=tl, dh=dh, pp=pp, wd=wd):
                    v.wait_ge(ts.h, wd)
                    dst = self.oacc[:, tl * 2048 + dh * 1024: tl * 2048 + (dh + 1) * 1024]
                    if grp == 0:
                        ev_done[(grp, idx)] = vs.inc(v.tensor_copy(dst, out_ps[pp]))
                    else:
                        ev_done[(grp, idx)] = vs.inc(v.tensor_tensor(dst, dst, out_ps[pp], OP.add))

        self._p5_half_ts = self.ts.n
        self._p5_half_vs = self.vs.n
        self._p5_tv_prev = t_evac[(NGRP - 1) * GA + GA - 1]
        self._p5_mul_prev = mul_done[(NGRP - 1) * GA + GA - 1]

        ffn_v = self.vs.n
        res_done = [0] * 4
        st_toks = [None] * 4
        for tl in range(4):
            xtok = [None]

            @b.sync
            def _(s, tl=tl, hf=hf, ffn_v=ffn_v):
                if tl >= 1:
                    s.wait_ge(vs.h, res_done[tl - 1])
                else:
                    s.wait_ge(vs.h, ffn_v)
                xtok[0] = self.ldq.issue(s, self.scrA[:, :], self.x2d[hf * 4 + tl])

            @b.vector
            def _(v, tl=tl, ffn_v=ffn_v):
                wait_toks(v, [xtok[0]])
                v.wait_ge(vs.h, ffn_v)
                if tl >= 1:
                    wait_toks(v, [st_toks[tl - 1]])

            @b.scalar
            def _(s, tl=tl, ffn_v=ffn_v):
                wait_toks(s, [xtok[0]])
                s.wait_ge(vs.h, ffn_v)
                if tl >= 1:
                    s.wait_ge(vs.h, res_done[tl - 1])

            src_t = self.oacc[:, tl * 2048:(tl + 1) * 2048]
            self.emit_ln(src_t, self.scrC[:, :], resid=self.scrA[:, :])
            res_done[tl] = vs.n
            rd = vs.n

            @b.gpsimd
            def _(g, tl=tl, hf=hf, rd=rd):
                g.wait_ge(vs.h, rd)
                st_toks[tl] = self.sgq.issue(g, self.y[hf * 4 + tl], self.scrC[:, :])

        self.barrier()

    def _finish(self):
        b = self.block
        vals = self.sgq.snapshot()

        @b.gpsimd
        def _(g):
            for h, n in vals:
                g.wait_ge(h, n)


def _hadamard(n):
    h = np.array([[1.0]], dtype=np.float64)
    while h.shape[0] < n:
        h = np.block([[h, h], [h, -h]])
    return h


def _build_consts():
    h64 = _hadamard(64).astype(f32)
    h128 = _hadamard(128).astype(f32)
    ident = np.eye(128, dtype=f32)
    kj = np.arange(128)[:, None]
    qi = np.arange(128)[None, :]
    tri = np.where(kj <= qi, 0.0, NEG).astype(f32)
    edge = np.where(kj > qi, 0.0, NEG).astype(f32)
    bias = np.ascontiguousarray(np.concatenate([tri, edge], axis=1))
    return h64, h128, ident, bias


def make_in_maps(x, wq, wk, wv, wo, w1, w2, w3):
    def t3(w, cols):
        return np.ascontiguousarray(np.ascontiguousarray(w.T).reshape(16, 128, cols))

    wqT, wkT, wvT, woT = t3(wq, D), t3(wk, D), t3(wv, D), t3(wo, D)
    w1T, w3T = t3(w1, HID), t3(w3, HID)
    w2r = np.ascontiguousarray(
        w2.reshape(D, 64, 128).transpose(1, 2, 0)).reshape(64, 128 * D)
    h64, h128, ident, bias = _build_consts()

    in_maps = []
    for core in range(8):
        bb, cc = divmod(core, 4)
        own = x[bb, cc * 1024:(cc + 1) * 1024]
        halo = (np.zeros((1024, D), f32) if cc == 0
                else x[bb, (cc - 1) * 1024: cc * 1024])
        xa = np.ascontiguousarray(
            np.concatenate([halo, own], axis=0)).reshape(NT, 128, D)
        valid = np.ones((128, 72), f32)
        if cc == 0:
            for qt in range(8):
                for kr in range(9):
                    if 8 + qt - kr < 8:
                        valid[:, qt * 9 + kr] = 0.0
        in_maps.append({
            "xall": xa,
            "wqT": wqT, "wkT": wkT, "wvT": wvT, "woT": woT,
            "w1T": w1T, "w3T": w3T, "w2r": w2r,
            "h64c": h64, "h128c": h128, "identc": ident,
            "biasc": bias, "validc": valid,
        })
    return in_maps


def kernel(L0_attn_x, L0_attn_Wq, L0_attn_Wk, L0_attn_Wv, L0_attn_Wo,
           L0_ffn_W1, L0_ffn_W2, L0_ffn_W3):
    x = np.asarray(L0_attn_x, dtype=f32)
    args = [np.asarray(a, dtype=f32) for a in
            (L0_attn_Wq, L0_attn_Wk, L0_attn_Wv, L0_attn_Wo,
             L0_ffn_W1, L0_ffn_W2, L0_ffn_W3)]
    wq, wk, wv, wo, w1, w2, w3 = args

    nc = Builder().build()
    in_maps = make_in_maps(x, wq, wk, wv, wo, w1, w2, w3)
    res = run_bass_kernel_spmd(nc, in_maps, core_ids=list(range(8)))
    out = np.empty((2, 4096, D), dtype=np.asarray(res.results[0]["y"]).dtype)
    for core in range(8):
        bb, cc = divmod(core, 4)
        out[bb, cc * 1024:(cc + 1) * 1024] = res.results[core]["y"].reshape(1024, D)
    return out
